# version 07
# baseline (speedup 1.0000x reference)
import contextlib
import numpy as np
import concourse.bass as bass
import concourse.mybir as mybir
from concourse.bass_utils import run_bass_kernel_spmd

F32 = mybir.dt.float32
BF16 = mybir.dt.bfloat16
I32 = mybir.dt.int32
AF = mybir.ActivationFunctionType
ALU = mybir.AluOpType
AX = mybir.AxisListType

D = 1024
EPS = 1e-5
SCALING = 0.125
N_CORES = 8

ENGINES = ("sync", "vector", "scalar", "tensor", "gpsimd")
NBANK = 4


class Prog:

    def __init__(self, nc, es):
        self.nc = nc
        self.tape = {e: [] for e in ENGINES}
        self.sems = {e: [es.enter_context(nc.semaphore(f"S{e}{i}")) for i in range(NBANK)]
                     for e in ENGINES}
        self.dsems = {e: [es.enter_context(nc.semaphore(f"D{e}{i}")) for i in range(8)]
                      for e in ("sync", "gpsimd", "scalar")}
        self.drr = {e: 0 for e in ("sync", "gpsimd", "scalar")}
        self.cnt = {}
        self.bank = 0
        self.writes = {}
        self.readers = {}
        self.waited = {e: {} for e in ENGINES}

    def rotate(self, i):
        self.bank = i % NBANK

    def _wait(self, eng, toks):
        need = {}
        for (s, v) in toks:
            if v > self.waited[eng].get(s, 0):
                need[s] = max(need.get(s, 0), v)
        for s in need:
            self.tape[eng].append(("w", s, need[s]))
            self.waited[eng][s] = need[s]

    def group(self, eng, fns, reads=(), writes=(), dma=False):
        toks = []
        for k in reads:
            t = self.writes.get(k)
            if t:
                toks.append(t)
        for k in writes:
            t = self.writes.get(k)
            if t:
                toks.append(t)
            toks.extend(self.readers.get(k, ()))
        self._wait(eng, toks)
        if dma:
            sem = self.dsems[eng][self.drr[eng] % 8]
            self.drr[eng] += 1
            prev = self.cnt.get(sem, 0)
            if prev:
                self._wait(eng, [(sem, prev)])
            n = 16
        else:
            sem = self.sems[eng][self.bank]
            n = 1
        self.cnt[sem] = self.cnt.get(sem, 0) + n
        tok = (sem, self.cnt[sem])
        last = len(fns) - 1
        for i, fn in enumerate(fns):
            self.tape[eng].append(("o", fn, sem if i == last else None, n))
        for k in reads:
            self.readers.setdefault(k, []).append(tok)
        for k in writes:
            self.writes[k] = tok
            self.readers[k] = []
        return tok

    def op(self, eng, fn, reads=(), writes=(), dma=False):
        return self.group(eng, [fn], reads, writes, dma)

    def barrier(self):
        toks = list(self.cnt.items())
        for e in ENGINES:
            self._wait(e, toks)

    def replay(self, block):
        def mk(e):
            def body(eng):
                for ent in self.tape[e]:
                    if ent[0] == "w":
                        eng.wait_ge(ent[1], ent[2])
                    else:
                        ins = ent[1](eng)
                        if ent[2] is not None:
                            ins.then_inc(ent[2], ent[3])
            return body
        block.sync(mk("sync"))
        block.vector(mk("vector"))
        block.scalar(mk("scalar"))
        block.tensor(mk("tensor"))
        block.gpsimd(mk("gpsimd"))


def build_program(n_seq=4, s_len=2048, t_tile=1024, debug_stop=None):
    nc = bass.Bass("TRN2", target_bir_lowering=False)
    KT = t_tile // 128
    SEGT = t_tile // 512
    TPS = s_len // t_tile
    n_tiles = n_seq * TPS
    assert t_tile % 512 == 0 and s_len % t_tile == 0

    x_in = nc.dram_tensor("x", [n_seq, s_len, D], F32, kind="ExternalInput").ap()
    y_out = nc.dram_tensor("y", [n_seq, s_len, D], F32, kind="ExternalOutput").ap()
    Wd = {w: nc.dram_tensor(w, [D, D], F32, kind="ExternalInput").ap()
          for w in ("wq", "wk", "wv", "wo")}
    Bd = {b: nc.dram_tensor(b, [1, D], F32, kind="ExternalInput").ap()
          for b in ("bq", "bk", "bv", "bo")}
    qw = nc.dram_tensor("qw", [6, D, D], I32, kind="ExternalInput").ap()
    scd = nc.dram_tensor("sc", [6, D, 64], F32, kind="ExternalInput").ap()
    fbias = nc.dram_tensor("fb", [6, D], F32, kind="ExternalInput").ap()
    lad = nc.dram_tensor("la", [6, 32, D], F32, kind="ExternalInput").ap()
    lbd = nc.dram_tensor("lb", [6, D, 32], F32, kind="ExternalInput").ap()
    gam = nc.dram_tensor("gam", [3, D], F32, kind="ExternalInput").ap()
    bet = nc.dram_tensor("bet", [3, D], F32, kind="ExternalInput").ap()
    idin = nc.dram_tensor("ident", [128, 128], F32, kind="ExternalInput").ap()
    cfh = nc.dram_tensor("cfh", [6, 128, 8192], BF16, kind="Internal").ap()
    cfl = nc.dram_tensor("cfl", [6, 128, 8192], BF16, kind="Internal").ap()
    cah = nc.dram_tensor("cah", [4, 128, 8192], BF16, kind="Internal").ap()
    cal = nc.dram_tensor("cal", [4, 128, 8192], BF16, kind="Internal").ap()
    bbch = nc.dram_tensor("bbch", [3, 128, 1024], F32, kind="Internal").ap()
    gbch = nc.dram_tensor("gbch", [3, 128, 1024], F32, kind="Internal").ap()
    bbch2 = nc.dram_tensor("bbch2", [3, 128, 1024], F32, kind="Internal").ap()

    R_BQ, R_BK, R_BV, R_BO = 0, 1, 2, 3
    R_FB = 4
    R_G = 10
    R_B = 13
    R_PH = 16
    R_ONES = 20

    es = contextlib.ExitStack()
    with es:
        P = Prog(nc, es)

        ident = es.enter_context(nc.sbuf_tensor("identt", [128, 128], F32))
        rowsbf = es.enter_context(nc.sbuf_tensor("rowsbf", [128, D], BF16))
        rowsbfl = es.enter_context(nc.sbuf_tensor("rowsbfl", [128, D], BF16))
        cstt = es.enter_context(nc.sbuf_tensor("cstt", [128, 4], F32))
        onesbf = es.enter_context(nc.sbuf_tensor("onesbf", [128, 128], BF16))
        ones32 = es.enter_context(nc.sbuf_tensor("ones32", [128, 128], F32))
        sgh = es.enter_context(nc.sbuf_tensor("sgh", [128, 1024], BF16))
        ps = [es.enter_context(nc.psum_tensor(f"ps{i}", [128, 512], F32)) for i in range(3)]
        pst = [es.enter_context(nc.psum_tensor(f"pst{i}", [128, 1024], F32)) for i in range(2)]
        psm = es.enter_context(nc.psum_tensor("psm", [128, 512], F32))

        pes = contextlib.ExitStack()
        rows32 = pes.enter_context(nc.sbuf_tensor("rows32", [128, D], F32))
        sg32 = pes.enter_context(nc.sbuf_tensor("sg32", [128, 1024], F32))
        asm = pes.enter_context(nc.sbuf_tensor("asm", [128, 8192], F32))
        slab_q = [pes.enter_context(nc.sbuf_tensor(f"slab_q{i}", [128, D], I32)) for i in range(2)]
        slab_f = [pes.enter_context(nc.sbuf_tensor(f"slab_f{i}", [128, D], F32)) for i in range(2)]
        sc_sl = [pes.enter_context(nc.sbuf_tensor(f"sc_sl{i}", [128, 64], F32)) for i in range(2)]
        se_sl = [pes.enter_context(nc.sbuf_tensor(f"se_sl{i}", [128, D], F32)) for i in range(2)]
        la_sb = pes.enter_context(nc.sbuf_tensor("la_sb", [32, D], F32))
        lbc_sb = pes.enter_context(nc.sbuf_tensor("lbc_sb", [128, 32], F32))
        lbT_sb = pes.enter_context(nc.sbuf_tensor("lbT_sb", [32, D], F32))
        whi_sb = pes.enter_context(nc.sbuf_tensor("whi_sb", [128, 8192], BF16))
        wlo_sb = pes.enter_context(nc.sbuf_tensor("wlo_sb", [128, 8192], BF16))

        _ps0 = [0]

        def next_ps0():
            _ps0[0] = (_ps0[0] + 1) % 3
            return _ps0[0]

        P.op("vector", lambda e: e.memset(rows32[:, :], 0.0), writes=[("rows32",)])
        P.op("sync", lambda e: e.dma_start(out=ident[:, :], in_=idin[:, :]),
             writes=[("ident",)], dma=True)
        row_loads = [("bq", R_BQ), ("bk", R_BK), ("bv", R_BV), ("bo", R_BO)]
        for nm, r in row_loads:
            P.op("sync", lambda e, nm=nm, r=r: e.dma_start(out=rows32[r:r + 1, :], in_=Bd[nm][:, :]),
                 writes=[("rows32",)], dma=True)
        for i in range(6):
            P.op("sync", lambda e, i=i: e.dma_start(out=rows32[R_FB + i:R_FB + i + 1, :],
                                                    in_=fbias[i:i + 1, :]),
                 writes=[("rows32",)], dma=True)
        for i in range(3):
            P.op("sync", lambda e, i=i: e.dma_start(out=rows32[R_G + i:R_G + i + 1, :],
                                                    in_=gam[i:i + 1, :]),
                 writes=[("rows32",)], dma=True)
            P.op("sync", lambda e, i=i: e.dma_start(out=rows32[R_B + i:R_B + i + 1, :],
                                                    in_=bet[i:i + 1, :]),
                 writes=[("rows32",)], dma=True)
        for j in range(4):
            for w, rsrc in ((0, R_BQ), (1, R_BK), (2, R_BV)):
                P.op("sync", lambda e, j=j, w=w, rsrc=rsrc: e.dma_start(
                    out=rows32[R_PH + j:R_PH + j + 1, 256 * w:256 * w + 256].rearrange(
                        "o (b u) -> o b u", u=64),
                    in_=rows32[rsrc:rsrc + 1, :].rearrange("o (b u) -> o b u", u=64)[:, j::4, :]),
                    reads=[("rows32",)], writes=[("rows32",)], dma=True)
        P.op("vector", lambda e: e.memset(ones32[:, :], 1.0), writes=[("ones32",)])
        P.op("vector", lambda e: e.memset(onesbf[:, :], 1.0), writes=[("onesbf",)])
        P.op("vector", lambda e: e.memset(cstt[:, 0:1], EPS), writes=[("cstt",)])
        P.op("vector", lambda e: e.tensor_copy(rowsbf[0:20, :], rows32[0:20, :]),
             reads=[("rows32",)], writes=[("rowsbf",)])
        P.op("vector", lambda e: e.scalar_tensor_tensor(
            rowsbfl[0:20, :], rows32[0:20, :], 1.0, rowsbf[0:20, :],
            op0=ALU.mult, op1=ALU.subtract),
            reads=[("rows32",), ("rowsbf",)], writes=[("rowsbfl",)])

        for i in range(6):
            for s in range(8):
                t = s % 2
                P.op("sync", lambda e, i=i, s=s, t=t: e.dma_start(
                    out=slab_q[t][:, :], in_=qw[i, 128 * s:128 * s + 128, :]),
                    writes=[("slab_q", t)], dma=True)
                P.op("sync", lambda e, i=i, s=s, t=t: e.dma_start(
                    out=sc_sl[t][:, :], in_=scd[i, 128 * s:128 * s + 128, :]),
                    writes=[("sc_sl", t)], dma=True)
                P.op("scalar", lambda e, t=t: e.activation(slab_f[t][:, :], slab_q[t][:, :],
                                                           AF.Copy, bias=-1.0, scale=1.0 / 7.5),
                     reads=[("slab_q", t)], writes=[("slab_f", t)])
                for r in range(16):
                    P.op("vector", lambda e, r=r, t=t: e.tensor_copy(
                        se_sl[t][:, r:1024:16], sc_sl[t][:, :]),
                        reads=[("sc_sl", t)], writes=[("se_sl", t)])
                P.op("vector", lambda e, t=t: e.tensor_tensor(
                    slab_f[t][:, :], slab_f[t][:, :], se_sl[t][:, :], op=ALU.mult),
                     reads=[("slab_f", t), ("se_sl", t)], writes=[("slab_f", t)])
                P.group("tensor",
                        [(lambda e, c=c, t=t: e.transpose(pst[t][:, 128 * c:128 * c + 128],
                                                          slab_f[t][:, 128 * c:128 * c + 128],
                                                          ident[:, :])) for c in range(8)],
                        reads=[("slab_f", t), ("ident",)], writes=[("pst", t)])
                P.op("vector", lambda e, s=s, t=t: e.tensor_copy(
                    asm[:, :].rearrange("p (c u) -> p c u", u=1024)[:, :, 128 * s:128 * s + 128],
                    pst[t][:, :].rearrange("p (c u) -> p c u", u=128)),
                    reads=[("pst", t)], writes=[("asm",)])
            P.op("sync", lambda e, i=i: e.dma_start(out=la_sb[:, :], in_=lad[i, :, :]),
                 writes=[("la_sb",)], dma=True)
            for c in range(8):
                P.op("sync", lambda e, i=i, c=c: e.dma_start(
                    out=lbc_sb[:, :], in_=lbd[i, 128 * c:128 * c + 128, :]),
                    writes=[("lbc",)], dma=True)
                P.op("tensor", lambda e: e.transpose(psm[0:32, 0:128], lbc_sb[:, :], ident[:, :]),
                     reads=[("lbc",), ("ident",)], writes=[("psm",)])
                P.op("vector", lambda e, c=c: e.tensor_copy(lbT_sb[:, 128 * c:128 * c + 128],
                                                            psm[0:32, 0:128]),
                     reads=[("psm",)], writes=[("lbT",)])
            for c in range(8):
                for oh in range(2):
                    jj = (2 * c + oh) % 3
                    P.op("tensor", lambda e, c=c, oh=oh, jj=jj: e.matmul(
                        ps[jj][:, 0:512], la_sb[:, 128 * c:128 * c + 128],
                        lbT_sb[:, 512 * oh:512 * oh + 512], start=True, stop=True),
                        reads=[("la_sb",), ("lbT",)], writes=[("ps", jj)])
                    P.op("vector", lambda e, c=c, oh=oh, jj=jj: e.scalar_tensor_tensor(
                        asm[:, 1024 * c + 512 * oh:1024 * c + 512 * oh + 512],
                        ps[jj][:, 0:512], 1.0,
                        asm[:, 1024 * c + 512 * oh:1024 * c + 512 * oh + 512],
                        op0=ALU.mult, op1=ALU.add),
                        reads=[("ps", jj), ("asm",)], writes=[("asm",)])
            P.op("vector", lambda e: e.tensor_copy(whi_sb[:, :], asm[:, :]),
                 reads=[("asm",)], writes=[("whi",)])
            P.op("vector", lambda e: e.scalar_tensor_tensor(
                wlo_sb[:, :], asm[:, :], 1.0, whi_sb[:, :], op0=ALU.mult, op1=ALU.subtract),
                reads=[("asm",), ("whi",)], writes=[("wlo",)])
            P.op("sync", lambda e, i=i: e.dma_start(out=cfh[i, :, :], in_=whi_sb[:, :]),
                 reads=[("whi",)], writes=[("cfh", i)], dma=True)
            P.op("sync", lambda e, i=i: e.dma_start(out=cfl[i, :, :], in_=wlo_sb[:, :]),
                 reads=[("wlo",)], writes=[("cfl", i)], dma=True)

        for wi, wn in enumerate(("wq", "wk", "wv", "wo")):
            P.op("sync", lambda e, wn=wn: e.dma_start(
                out=asm[:, :].rearrange("p (c u) -> p c u", u=1024),
                in_=Wd[wn].rearrange("(c p) o -> p c o", p=128)),
                writes=[("asm",)], dma=True)
            P.op("vector", lambda e: e.tensor_copy(whi_sb[:, :], asm[:, :]),
                 reads=[("asm",)], writes=[("whi",)])
            P.op("vector", lambda e: e.scalar_tensor_tensor(
                wlo_sb[:, :], asm[:, :], 1.0, whi_sb[:, :], op0=ALU.mult, op1=ALU.subtract),
                reads=[("asm",), ("whi",)], writes=[("wlo",)])
            for j in range(4):
                for hilo, (csrc, cdst) in enumerate(((whi_sb, cah), (wlo_sb, cal))):
                    rk = ("whi",) if hilo == 0 else ("wlo",)
                    if wn != "wo":
                        P.op("sync", lambda e, j=j, wi=wi, csrc=csrc, cdst=cdst: e.dma_start(
                            out=cdst[j, :, 2048 * wi:2048 * wi + 2048].rearrange(
                                "p (c b u) -> p c b u", c=8, b=4),
                            in_=csrc[:, :].rearrange(
                                "p (c b g u) -> p c b g u", c=8, b=4, g=4)[:, :, :, j, :]),
                            reads=[rk], writes=[("cache_a", j)], dma=True)
                    else:
                        for b in range(4):
                            chunk = (256 * b + 64 * j) // 128
                            poff = (256 * b + 64 * j) % 128
                            c2, phh = b // 2, b % 2
                            P.op("sync", lambda e, j=j, b=b, csrc=csrc, cdst=cdst,
                                 chunk=chunk, poff=poff, c2=c2, phh=phh: e.dma_start(
                                     out=cdst[j, 64 * phh:64 * phh + 64,
                                              6144 + 1024 * c2:6144 + 1024 * c2 + 1024],
                                     in_=csrc[poff:poff + 64, 1024 * chunk:1024 * chunk + 1024]),
                                 reads=[rk], writes=[("cache_a", j)], dma=True)

        for s3 in range(3):
            for row, cache in ((R_G + s3, gbch), (R_B + s3, bbch2), (R_FB + 2 * s3, bbch)):
                P.op("sync", lambda e, row=row: e.dma_start(out=sg32[0:1, :],
                                                            in_=rows32[row:row + 1, :]),
                     reads=[("rows32",)], writes=[("sg32",)], dma=True)
                for oh in range(2):
                    jj = next_ps0()
                    P.op("tensor", lambda e, oh=oh, jj=jj: e.matmul(
                        ps[jj][:, 0:512], ones32[0:1, 0:128],
                        sg32[0:1, 512 * oh:512 * oh + 512], start=True, stop=True),
                        reads=[("ones32",), ("sg32",)], writes=[("ps", jj)])
                    P.op("scalar", lambda e, jj=jj, oh=oh: e.copy(
                        slab_f[0][:, 512 * oh:512 * oh + 512], ps[jj][:, 0:512]),
                        reads=[("ps", jj)], writes=[("slab_f", 0)])
                P.op("sync", lambda e, s3=s3, cache=cache: e.dma_start(
                    out=cache[s3, :, :], in_=slab_f[0][:, :]),
                    reads=[("slab_f", 0)], writes=[("bc_cache", id(cache), s3)], dma=True)
        pes.close()

        xb = es.enter_context(nc.sbuf_tensor("xb", [128, KT * 1024], F32))
        hb = es.enter_context(nc.sbuf_tensor("hb", [128, KT * 1024], F32))
        y1 = es.enter_context(nc.sbuf_tensor("y1", [128, KT * 1024], F32))
        gsl = [es.enter_context(nc.sbuf_tensor(f"g{i}", [128, 1024], F32)) for i in range(2)]
        gbc = es.enter_context(nc.sbuf_tensor("gbc", [128, 1024], F32))
        bbc = es.enter_context(nc.sbuf_tensor("bbc", [128, 1024], F32))
        b1bc = es.enter_context(nc.sbuf_tensor("b1bc", [128, 1024], F32))
        aTh = [es.enter_context(nc.sbuf_tensor(f"aTh{i}", [128, 1024], BF16)) for i in range(4)]
        aTl = [es.enter_context(nc.sbuf_tensor(f"aTl{i}", [128, 1024], BF16)) for i in range(4)]
        wah = es.enter_context(nc.sbuf_tensor("wah", [128, 2 * 8192], BF16))
        wal = es.enter_context(nc.sbuf_tensor("wal", [128, 2 * 8192], BF16))
        stt = es.enter_context(nc.sbuf_tensor("stt", [128, 64], F32))
        scT = es.enter_context(nc.sbuf_tensor("scT", [64, 128], F32))
        smst = es.enter_context(nc.sbuf_tensor("smst", [64, 8], F32))

        P.barrier()


        ps_rr = [0]
        pst_rr = [0]

        def next_ps():
            ps_rr[0] = (ps_rr[0] + 1) % 3
            return ps_rr[0]

        def next_pst():
            pst_rr[0] = (pst_rr[0] + 1) % 2
            return pst_rr[0]

        sg_rr = [0]

        def stage_bias(row):
            base = (0, 32, 64)[sg_rr[0] % 3]
            sg_rr[0] += 1
            P.op("sync", lambda e, row=row, base=base: e.dma_start(
                out=sgh[base:base + 1, :], in_=rowsbf[row:row + 1, :]),
                reads=[("rowsbf",)], writes=[("sgh", base)], dma=True)
            P.op("sync", lambda e, row=row, base=base: e.dma_start(
                out=sgh[base + 1:base + 2, :], in_=rowsbfl[row:row + 1, :]),
                reads=[("rowsbfl",)], writes=[("sgl", base)], dma=True)
            return base

        def xsl(k):
            return xb[:, 1024 * k:1024 * k + 1024]

        def qkv_ap(s2):
            return y1[:, 1024 * (KT - 2 + s2):1024 * (KT - 2 + s2) + 768]

        def opk_ap(s2):
            return y1[:, 1024 * (KT - 2 + s2) + 768:1024 * (KT - 2 + s2) + 1024]

        def akey(s2):
            return ("y1", KT - 2 + s2)

        def hsl(k):
            return hb[:, 1024 * k:1024 * k + 1024]

        def y1sl(k):
            return y1[:, 1024 * k:1024 * k + 1024]

        def y1key(k):
            return ("y1", k)

        def trans_split(src_fn, src_keys, slot, nch=8):
            t = next_pst()
            P.group("tensor",
                    [(lambda e, c=c, t=t: e.transpose(pst[t][:, 128 * c:128 * c + 128],
                                                      src_fn(c), ident[:, :])) for c in range(nch)],
                    reads=list(src_keys) + [("ident",)], writes=[("pst", t)])
            P.op("scalar", lambda e, t=t, nch=nch, slot=slot: e.activation(
                aTh[slot][:, 0:128 * nch], pst[t][:, 0:128 * nch], AF.Copy),
                reads=[("pst", t)], writes=[("aTh", slot)])
            P.op("vector", lambda e, t=t, nch=nch, slot=slot: e.scalar_tensor_tensor(
                aTl[slot][:, 0:128 * nch], pst[t][:, 0:128 * nch], 1.0,
                aTh[slot][:, 0:128 * nch], op0=ALU.mult, op1=ALU.subtract),
                reads=[("pst", t), ("aTh", slot)], writes=[("aTl", slot)])

        def mm_group(slot, wpar, woff_fn, n, bias_row, bias_off, nch=8):
            j = next_ps()
            fns = []
            for c in range(nch):
                for pi, (lhs, rhs) in enumerate(((aTh, wah), (aTh, wal), (aTl, wah))):
                    fns.append(lambda e, c=c, lhs=lhs, rhs=rhs, j=j, slot=slot,
                               wo=8192 * wpar + woff_fn(c), n=n, first=(c == 0 and pi == 0):
                               e.matmul(ps[j][:, 0:n], lhs[slot][:, 128 * c:128 * c + 128],
                                        rhs[:, wo:wo + n], start=first, stop=False))
            if bias_row is not None:
                base = bias_row
                fns.append(lambda e, j=j, n=n, boff=bias_off, base=base:
                           e.matmul(ps[j][:, 0:n], onesbf[base:base + 2, 0:128],
                                    sgh[base:base + 2, boff:boff + n], start=False, stop=True))
            else:
                fns[-1] = (lambda f: (lambda e: f(e, stop_override=True)))(None)
                c = nch - 1
                fns[-1] = (lambda e, c=c, j=j, slot=slot, wo=8192 * wpar + woff_fn(c), n=n:
                           e.matmul(ps[j][:, 0:n], aTl[slot][:, 128 * c:128 * c + 128],
                                    wah[:, wo:wo + n], start=False, stop=True))
            rds = [("aTh", slot), ("aTl", slot), ("wh", wpar), ("wl", wpar)]
            if bias_row is not None:
                rds += [("onesbf",), ("sgh", bias_row), ("sgl", bias_row)]
            P.group("tensor", fns, reads=rds, writes=[("ps", j)])
            return j

        wl = {"order": [], "pos": 0, "glob": 0, "pars": []}

        def wl_reset():
            wl["order"] = ([("f", i) for i in range(6)] + [("a", j) for j in range(4)]
                           + [("f", i) for i in range(6)])
            wl["pos"] = 0

        def wl_pump():
            if wl["pos"] >= len(wl["order"]):
                return
            kind, i = wl["order"][wl["pos"]]
            wl["pos"] += 1
            par = wl["glob"] % 2
            wl["glob"] += 1
            hi = cfh[i, :, :] if kind == "f" else cah[i, :, :]
            lo = cfl[i, :, :] if kind == "f" else cal[i, :, :]
            P.op("gpsimd", lambda e, hi=hi, par=par: e.dma_start(
                out=wah[:, 8192 * par:8192 * par + 8192], in_=hi),
                writes=[("wh", par)], dma=True)
            P.op("gpsimd", lambda e, lo=lo, par=par: e.dma_start(
                out=wal[:, 8192 * par:8192 * par + 8192], in_=lo),
                writes=[("wl", par)], dma=True)
            wl["pars"].append(par)

        def wl_par():
            return wl["pars"].pop(0)

        def stats(src, src_key, sb):
            P.op("vector", lambda e: e.reduce_sum(stt[:, sb:sb + 1], src, axis=AX.X),
                 reads=[src_key], writes=[("st", sb, 0)])
            P.op("scalar", lambda e: e.activation(gsl[0][:, :], src, AF.Square,
                                                  accum_out=stt[:, sb + 1:sb + 2]),
                 reads=[src_key], writes=[("g", 0), ("st", sb, 1)])
            P.op("vector", lambda e: e.tensor_scalar(stt[:, sb + 2:sb + 3], stt[:, sb:sb + 1],
                                                     -1.0 / D, None, op0=ALU.mult),
                 reads=[("st", sb, 0)], writes=[("st", sb, 2)])
            P.op("scalar", lambda e: e.activation(stt[:, sb + 3:sb + 4], stt[:, sb + 2:sb + 3],
                                                  AF.Square),
                 reads=[("st", sb, 2)], writes=[("st", sb, 3)])
            P.op("vector", lambda e: e.tensor_scalar(stt[:, sb + 4:sb + 5], stt[:, sb + 1:sb + 2],
                                                     1.0 / D, stt[:, sb + 3:sb + 4],
                                                     op0=ALU.mult, op1=ALU.subtract),
                 reads=[("st", sb, 1), ("st", sb, 3)], writes=[("st", sb, 4)])
            P.op("scalar", lambda e: e.activation(stt[:, sb + 5:sb + 6], stt[:, sb + 4:sb + 5],
                                                  AF.Ln, bias=cstt[:, 0:1]),
                 reads=[("st", sb, 4), ("cstt",)], writes=[("st", sb, 5)])
            P.op("scalar", lambda e: e.activation(stt[:, sb + 6:sb + 7], stt[:, sb + 5:sb + 6],
                                                  AF.Exp, scale=-0.5),
                 reads=[("st", sb, 5)], writes=[("st", sb, 6)])

        def ln_apply(dst, src, src_key, dst_key, sb):
            P.op("vector", lambda e: e.tensor_scalar(dst, src, stt[:, sb + 2:sb + 3],
                                                     stt[:, sb + 6:sb + 7],
                                                     op0=ALU.add, op1=ALU.mult),
                 reads=[src_key, ("st", sb, 2), ("st", sb, 6)], writes=[dst_key])

        def gb_build(blk):
            P.op("sync", lambda e, blk=blk: e.dma_start(out=gbc[:, :], in_=gbch[blk, :, :]),
                 writes=[("gbc",)], dma=True)
            P.op("sync", lambda e, blk=blk: e.dma_start(out=bbc[:, :], in_=bbch2[blk, :, :]),
                 writes=[("bbc",)], dma=True)

        def ffn_pass(pre=None, finalize=None):
            for s in range(3):
                l1, l2 = 2 * s, 2 * s + 1
                gb_build(s)
                P.op("sync", lambda e, s=s: e.dma_start(out=b1bc[:, :], in_=bbch[s, :, :]),
                     reads=[], writes=[("b1bc",)], dma=True)
                b2 = stage_bias(R_FB + l2)
                par1 = wl_par()
                if s == 0 and pre is not None:
                    pre(0)
                    pre(1)
                if s == 0:
                    trans_split(lambda c: hsl(0)[:, 128 * c:128 * c + 128], [("h", 0)], 0)
                    trans_split(lambda c: hsl(1)[:, 128 * c:128 * c + 128], [("h", 1)], 1)
                for k in range(KT):
                    slot = k % 2
                    sb = 16 * (k % 4)
                    if s == 0 and pre is not None and k + 2 < KT:
                        pre(k + 2)
                    for oh in range(2):
                        j = mm_group(slot, par1, lambda c, oh=oh: 1024 * c + 512 * oh, 512,
                                     None, 0)
                        P.op("vector", lambda e, j=j, k=k, oh=oh: e.scalar_tensor_tensor(
                            y1sl(k)[:, 512 * oh:512 * oh + 512], ps[j][:, 0:512], 1.0,
                            b1bc[:, 512 * oh:512 * oh + 512], op0=ALU.mult, op1=ALU.add),
                            reads=[("ps", j), ("b1bc",)], writes=[y1key(k)])
                    if k + 2 < KT:
                        trans_split(lambda c, k=k: hsl(k + 2)[:, 128 * c:128 * c + 128],
                                    [("h", k + 2)], (k + 2) % 2)
                    stats(y1sl(k), y1key(k), sb)
                    ln_apply(y1sl(k), y1sl(k), y1key(k), y1key(k), sb)
                    P.op("vector", lambda e, k=k: e.tensor_tensor(y1sl(k), y1sl(k), gbc[:, :],
                                                                 op=ALU.mult),
                         reads=[y1key(k), ("gbc",)], writes=[y1key(k)])
                    P.op("vector", lambda e, k=k: e.tensor_tensor(y1sl(k), y1sl(k), bbc[:, :],
                                                                 op=ALU.add),
                         reads=[y1key(k), ("bbc",)], writes=[y1key(k)])
                wl_pump()
                par2 = wl_par()
                P.op("scalar", lambda e: e.activation(gsl[0][:, :], y1sl(0), AF.Gelu),
                     reads=[y1key(0)], writes=[("g", 0)])
                trans_split(lambda c: gsl[0][:, 128 * c:128 * c + 128], [("g", 0)], 2)
                for k in range(KT):
                    slot = k % 2
                    if k + 1 < KT:
                        P.op("scalar", lambda e, k=k: e.activation(
                            gsl[(k + 1) % 2][:, :], y1sl(k + 1), AF.Gelu),
                            reads=[y1key(k + 1)], writes=[("g", (k + 1) % 2)])
                        trans_split(lambda c, k=k: gsl[(k + 1) % 2][:, 128 * c:128 * c + 128],
                                    [("g", (k + 1) % 2)], 2 + (k + 1) % 2)
                    for oh in range(2):
                        j = mm_group(2 + slot, par2, lambda c, oh=oh: 1024 * c + 512 * oh, 512,
                                     b2, 512 * oh)
                        P.op("vector", lambda e, j=j, k=k, oh=oh: e.scalar_tensor_tensor(
                            hsl(k)[:, 512 * oh:512 * oh + 512], ps[j][:, 0:512], 1.0,
                            hsl(k)[:, 512 * oh:512 * oh + 512], op0=ALU.mult, op1=ALU.add),
                            reads=[("ps", j), ("h", k)], writes=[("h", k)])
                    if s < 2 and k < 2:
                        trans_split(lambda c, k=k: hsl(k)[:, 128 * c:128 * c + 128],
                                    [("h", k)], k % 2)
                    if s == 2 and finalize is not None:
                        finalize(k)
                wl_pump()

        def ln_k(k):
            sb = 16 * (k % 4)
            stats(xsl(k), ("x", k), sb)
            ln_apply(hsl(k), xsl(k), ("x", k), ("h", k), sb)

        def half_add(k):
            P.op("vector", lambda e, k=k: e.scalar_tensor_tensor(
                xsl(k), hsl(k), 0.5, xsl(k), op0=ALU.mult, op1=ALU.add),
                reads=[("h", k), ("x", k)], writes=[("x", k)])

        def attention(tile, pre=None, dbg=None, seq=0, tbase=0):
            a2 = [0]

            def stage_a(k, parw, bph, j):
                s2 = a2[0] % 2
                a2[0] += 1
                slot = k % 2
                trans_split(lambda c, k=k: hsl(k)[:, 128 * c:128 * c + 128], [("h", k)], slot)
                jq = mm_group(slot, parw, lambda c: 256 * c, 256, bph, 0)
                jk = mm_group(slot, parw, lambda c: 2048 + 256 * c, 256, bph, 256)
                jv = mm_group(slot, parw, lambda c: 4096 + 256 * c, 256, bph, 512)
                for (jj, off) in ((jq, 0), (jk, 256), (jv, 512)):
                    P.op("scalar", lambda e, jj=jj, off=off, s2=s2: e.copy(
                        qkv_ap(s2)[:, off:off + 256], ps[jj][:, 0:256]),
                        reads=[("ps", jj)], writes=[akey(s2)])
                if dbg == "qdbg":
                    P.op("sync", lambda e, s2=s2, k=k, seq=seq, tbase=tbase: e.dma_start(
                        out=y_out[seq, tbase + 128 * k:tbase + 128 * k + 128, 0:768],
                        in_=qkv_ap(s2)[:, 0:768]),
                        reads=[akey(s2)], dma=True)
                return s2

            def stage_b(k, s2, parw, bo_base):
                slot = k % 2
                sb = 16 * (k % 4)
                P.op("vector", lambda e, s2=s2: e.tensor_tensor(
                    opk_ap(s2)[:, :], qkv_ap(s2)[:, 0:256], qkv_ap(s2)[:, 256:512],
                    op=ALU.mult),
                    reads=[akey(s2)], writes=[akey(s2)])
                P.op("vector", lambda e, s2=s2, sb=sb: e.reduce_sum(
                    stt[:, sb + 8:sb + 12],
                    opk_ap(s2)[:, :].rearrange("p (h u) -> p h u", u=64), axis=AX.X),
                    reads=[akey(s2)], writes=[("st", sb, 8)])
                P.op("tensor", lambda e, sb=sb: e.transpose(
                    psm[0:4, 0:128], stt[:, sb + 8:sb + 12], ident[:, :]),
                    reads=[("st", sb, 8), ("ident",)], writes=[("psm",)])
                P.op("scalar", lambda e, s2=s2: e.activation(
                    scT[32 * s2:32 * s2 + 4, :], psm[0:4, 0:128], AF.Copy, scale=SCALING),
                    reads=[("psm",)], writes=[("scT", s2)])
                P.op("vector", lambda e, s2=s2: e.reduce_max(
                    smst[32 * s2:32 * s2 + 4, 0:1], scT[32 * s2:32 * s2 + 4, :], axis=AX.X,
                    negate=True),
                    reads=[("scT", s2)], writes=[("sm", s2, 0)])
                P.op("scalar", lambda e, s2=s2: e.activation(
                    scT[32 * s2:32 * s2 + 4, :], scT[32 * s2:32 * s2 + 4, :], AF.Exp,
                    bias=smst[32 * s2:32 * s2 + 4, 0:1],
                    accum_out=smst[32 * s2:32 * s2 + 4, 1:2]),
                    reads=[("scT", s2), ("sm", s2, 0)], writes=[("scT", s2), ("sm", s2, 1)])
                P.op("vector", lambda e, s2=s2: e.reciprocal(
                    smst[32 * s2:32 * s2 + 4, 2:3], smst[32 * s2:32 * s2 + 4, 1:2]),
                    reads=[("sm", s2, 1)], writes=[("sm", s2, 2)])
                P.op("vector", lambda e, s2=s2: e.tensor_scalar(
                    scT[32 * s2:32 * s2 + 4, :], scT[32 * s2:32 * s2 + 4, :],
                    smst[32 * s2:32 * s2 + 4, 2:3], None, op0=ALU.mult),
                    reads=[("scT", s2), ("sm", s2, 2)], writes=[("scT", s2)])
                P.op("tensor", lambda e, s2=s2: e.transpose(
                    psm[:, 128:132], scT[32 * s2:32 * s2 + 4, :],
                    ident[32 * s2:32 * s2 + 4, 32 * s2:32 * s2 + 4]),
                    reads=[("scT", s2), ("ident",)], writes=[("psm",)])
                P.op("scalar", lambda e, sb=sb: e.copy(stt[:, sb + 8:sb + 12], psm[:, 128:132]),
                     reads=[("psm",)], writes=[("st", sb, 8)])
                if dbg == "pdbg":
                    P.op("sync", lambda e, sb=sb, k=k, seq=seq, tbase=tbase: e.dma_start(
                        out=y_out[seq, tbase + 128 * k:tbase + 128 * k + 128, 0:4],
                        in_=stt[:, sb + 8:sb + 12]),
                        reads=[("st", sb, 8)], dma=True)
                    return
                for b in range(4):
                    P.op("vector", lambda e, s2=s2, sb=sb, b=b: e.tensor_scalar(
                        opk_ap(s2)[:, 64 * b:64 * b + 64],
                        qkv_ap(s2)[:, 512 + 64 * b:512 + 64 * b + 64],
                        stt[:, sb + 8 + b:sb + 9 + b], None, op0=ALU.mult),
                        reads=[akey(s2), ("st", sb, 8)], writes=[akey(s2)])
                if dbg == "odbg":
                    P.op("sync", lambda e, s2=s2, k=k, seq=seq, tbase=tbase: e.dma_start(
                        out=y_out[seq, tbase + 128 * k:tbase + 128 * k + 128, 0:256],
                        in_=opk_ap(s2)[:, 0:256]),
                        reads=[akey(s2)], dma=True)
                    return
                trans_split(lambda c, s2=s2: opk_ap(s2)[:, 128 * c:128 * c + 128],
                            [akey(s2)], slot, nch=2)
                for oh in range(2):
                    j2 = mm_group(slot, parw, lambda c, oh=oh: 6144 + 1024 * c + 512 * oh, 512,
                                  bo_base, 512 * oh, nch=2)
                    P.op("vector", lambda e, j2=j2, k=k, oh=oh: e.scalar_tensor_tensor(
                        xsl(k)[:, 512 * oh:512 * oh + 512], ps[j2][:, 0:512],
                        1.0, xsl(k)[:, 512 * oh:512 * oh + 512], op0=ALU.mult, op1=ALU.add),
                        reads=[("ps", j2), ("x", k)], writes=[("x", k)])

            def stage_at(row, base):
                P.op("sync", lambda e, row=row, base=base: e.dma_start(
                    out=sgh[base:base + 1, :], in_=rowsbf[row:row + 1, :]),
                    reads=[("rowsbf",)], writes=[("sgh", base)], dma=True)
                P.op("sync", lambda e, row=row, base=base: e.dma_start(
                    out=sgh[base + 1:base + 2, :], in_=rowsbfl[row:row + 1, :]),
                    reads=[("rowsbfl",)], writes=[("sgl", base)], dma=True)

            stage_at(R_BO, 64)
            bo_base = 64
            stage_at(R_PH + 0, 0)
            if pre is not None:
                for k in [4 * sg for sg in range(SEGT)]:
                    pre(k)
            pend = []
            pops = [0]

            def pop_b():
                stage_b(*pend.pop(0))
                pops[0] += 1
                if pops[0] % SEGT == 0:
                    wl_pump()

            for j in range(4):
                parw = wl_par()
                bph = 32 * (j % 2)
                if j + 1 < 4:
                    stage_at(R_PH + j + 1, 32 * ((j + 1) % 2))
                ks = [4 * sg + j for sg in range(SEGT)]
                nxt = [4 * sg + j + 1 for sg in range(SEGT)] if j + 1 < 4 else []
                for i, k in enumerate(ks):
                    if len(pend) >= SEGT and dbg != "qdbg":
                        pop_b()
                    s2 = stage_a(k, parw, bph, j)
                    if pre is not None and i < len(nxt):
                        pre(nxt[i])
                    if dbg == "qdbg":
                        continue
                    pend.append((k, s2, parw, bo_base))
                if dbg == "qdbg":
                    wl_pump()
            while pend:
                pop_b()

        def debug_store(seq, tbase, which):
            for k in range(KT):
                src_ap = xsl(k) if which == "x" else hsl(k)
                key = ("x", k) if which == "x" else ("h", k)
                P.op("sync", lambda e, seq=seq, tbase=tbase, k=k, src_ap=src_ap: e.dma_start(
                    out=y_out[seq, tbase + 128 * k:tbase + 128 * k + 128, :], in_=src_ap),
                    reads=[key], dma=True)

        for tile in range(n_tiles):
            P.rotate(tile)
            seq = tile // TPS
            tbase = (tile % TPS) * t_tile
            wl_reset()
            wl_pump()
            wl_pump()
            for k in range(KT):
                segg = (tbase // 512) + k // 4
                jph = k % 4
                st0 = 512 * segg + jph
                P.op("sync", lambda e, seq=seq, st0=st0, k=k: e.dma_start(
                    out=xsl(k), in_=x_in[seq, st0:st0 + 509:4, :]),
                    writes=[("x", k)], dma=True)
            stages = [("load", "x"), ("ln1", "h"), ("ffn1", "h"), ("xadd1", "x"),
                      ("ln2", "h"), ("attn", "x"), ("ln3", "h"), ("ffn2", "h"),
                      ("xadd2", "x")]
            def stop_after(name, which):
                if debug_stop == name:
                    debug_store(seq, tbase, which)
                    return True
                return False
            if stop_after("load", "x"):
                continue
            if debug_stop == "ln1":
                for k in range(KT):
                    ln_k(k)
                debug_store(seq, tbase, "h")
                continue
            ffn_pass(pre=ln_k)
            if stop_after("ffn1", "h"):
                continue
            if debug_stop in ("xadd1", "ln2"):
                for k in range(KT):
                    half_add(k)
                if stop_after("xadd1", "x"):
                    continue
                for k in range(KT):
                    ln_k(k)
                debug_store(seq, tbase, "h")
                continue

            def pre_attn(k):
                half_add(k)
                ln_k(k)

            if debug_stop in ("qdbg", "pdbg", "odbg"):
                attention(tile, pre=pre_attn, dbg=debug_stop, seq=seq, tbase=tbase)
                continue
            attention(tile, pre=pre_attn)
            if stop_after("attn", "x"):
                continue
            if debug_stop == "ln3":
                for k in range(KT):
                    ln_k(k)
                debug_store(seq, tbase, "h")
                continue
            def fin(k, seq=seq, tbase=tbase):
                half_add(k)
                segg = (tbase // 512) + k // 4
                jph = k % 4
                st0 = 512 * segg + jph
                P.op("sync", lambda e, seq=seq, st0=st0, k=k: e.dma_start(
                    out=y_out[seq, st0:st0 + 509:4, :], in_=xsl(k)),
                    reads=[("x", k)], dma=True)
            if debug_stop == "ffn2":
                ffn_pass(pre=ln_k)
                debug_store(seq, tbase, "h")
                continue
            ffn_pass(pre=ln_k, finalize=fin)

        P.barrier()

        block = es.enter_context(nc.Block())
        P.replay(block)

    return nc


_CACHE = {}


def _get_program(n_seq, s_len, t_tile):
    key = (n_seq, s_len, t_tile)
    if key not in _CACHE:
        _CACHE[key] = build_program(n_seq, s_len, t_tile)
    return _CACHE[key]


def kernel(L0_attn_x, L0_attn_Wq, L0_attn_bq, L0_attn_Wk, L0_attn_bk, L0_attn_Wv,
           L0_attn_bv, L0_attn_Wo, L0_attn_bo, L0_ffn_qweights, L0_ffn_scales,
           L0_ffn_biases, L0_ffn_lora_a, L0_ffn_lora_b, L0_ffn_ln_gamma, L0_ffn_ln_beta):
    x = np.ascontiguousarray(np.asarray(L0_attn_x, dtype=np.float32))
    B, S, Dm = x.shape
    per = B // N_CORES
    nc = _get_program(per, S, 1024)
    common = {
        "wq": np.ascontiguousarray(np.asarray(L0_attn_Wq, np.float32)),
        "wk": np.ascontiguousarray(np.asarray(L0_attn_Wk, np.float32)),
        "wv": np.ascontiguousarray(np.asarray(L0_attn_Wv, np.float32)),
        "wo": np.ascontiguousarray(np.asarray(L0_attn_Wo, np.float32)),
        "bq": np.asarray(L0_attn_bq, np.float32).reshape(1, Dm),
        "bk": np.asarray(L0_attn_bk, np.float32).reshape(1, Dm),
        "bv": np.asarray(L0_attn_bv, np.float32).reshape(1, Dm),
        "bo": np.asarray(L0_attn_bo, np.float32).reshape(1, Dm),
        "qw": np.ascontiguousarray(np.asarray(L0_ffn_qweights, np.int32)),
        "sc": np.ascontiguousarray(np.asarray(L0_ffn_scales, np.float32)),
        "fb": np.ascontiguousarray(np.asarray(L0_ffn_biases, np.float32)),
        "la": np.ascontiguousarray(np.asarray(L0_ffn_lora_a, np.float32)),
        "lb": np.ascontiguousarray(np.asarray(L0_ffn_lora_b, np.float32)),
        "gam": np.ascontiguousarray(np.asarray(L0_ffn_ln_gamma, np.float32)),
        "bet": np.ascontiguousarray(np.asarray(L0_ffn_ln_beta, np.float32)),
        "ident": np.eye(128, dtype=np.float32),
    }
    in_maps = [dict(common, x=np.ascontiguousarray(x[c * per:(c + 1) * per]))
               for c in range(N_CORES)]
    res = run_bass_kernel_spmd(nc, in_maps, core_ids=list(range(N_CORES)))
    return np.concatenate([np.asarray(res.results[c]["y"]) for c in range(N_CORES)], axis=0)
